# version 02 (final)
# baseline (speedup 1.0000x reference)
import math
from contextlib import ExitStack

import numpy as np
import concourse.bass as bass
import concourse.mybir as mybir

F32 = mybir.dt.float32
BF16 = mybir.dt.bfloat16
I32 = mybir.dt.int32
A = mybir.AluOpType
AF = mybir.ActivationFunctionType
AX = mybir.AxisListType.X
DS_ = bass.ds
NEG = -1e30
EPS_LN = 1e-5
EPS_RMS = 1e-6


class CFG:
    def __init__(self, D=1024, T=4096, H=16, DI=2048, E=8, HFF=4096,
                 QG=512, HQ=1024, TG=1024):
        self.D, self.T, self.H, self.DI, self.E, self.HFF = D, T, H, DI, E, HFF
        self.DSt = 16
        self.HD = 64
        self.QG, self.HQ, self.TG = QG, HQ, TG
        self.HALF = T // 2
        self.DIm = DI // 2
        self.CH = 512
        assert self.HALF % self.CH == 0 and self.HALF % QG == 0 and QG % 128 == 0
        assert D % 128 == 0 and self.HALF % TG == 0 and TG % self.CH == 0
        assert HFF % HQ == 0 and HQ % 128 == 0 and self.DIm % 128 == 0
        self.NT = self.HALF // 128
        self.NCH = self.HALF // self.CH
        self.NKC = self.T // self.CH
        self.NQG = self.HALF // QG
        self.QB = QG // 128
        self.DT = D // 128
        self.NHQ = HFF // HQ
        self.HQT = HQ // 128
        self.DIT = self.DIm // 128
        self.DIT2 = DI // 128
        self.KBUF = self.HALF + T
        self.KQRC = 2 * H * 128
        self.VSTC = H * 65
        self.NTG = self.HALF // TG
        self.NST = D // 512
        self.HB = H // 8


def T3(src2d):
    return src2d.rearrange("(a q) k -> q a k", q=128)


def S3(dst2d, k):
    return dst2d.rearrange("q (a k) -> q a k", k=k)


SELROW = {"obb": 0, "slo": 1, "shi": 2, "p0": 3, "p1": 4, "orow": 5, "orowh": 6}


def hv(t2d, width, lo, hi, nh=8):
    return t2d.rearrange("q (hh x) -> q hh x", x=width)[:, :, lo:hi]


class Prog:
    ENGINES = ("tensor", "vector", "scalar", "gpsimd", "sync")

    def __init__(self, nc):
        self.nc = nc
        self.q = {e: [] for e in self.ENGINES}
        self.ctr = {}
        self.sems = {}
        self.regs = {}

    def canon(self, name):
        if "bar" in name:
            return "bar"
        if name in ("f1ag", "f2ag", "mag2", "mag3", "mag4"):
            return "ag"
        if name == "mhl0":
            return "hl0"
        for pre in ("f1", "f2", "m"):
            if name.startswith(pre) and len(name) > len(pre) and name[len(pre)].isupper():
                name = name[len(pre):]
                break
        if name and name[0] in "ABCDPMSOW" and len(name) > 1:
            name = name[1:]
        return name

    def sem(self, name):
        name = self.canon(name)
        if name not in self.sems:
            self.sems[name] = self.nc.ctx.enter_context(self.nc.semaphore(name))
            self.ctr[name] = 0
        return self.sems[name]

    def emit(self, eng, fn):
        self.q[eng].append(fn)

    def op(self, eng, fn, inc=None, n=1):
        if inc is not None:
            s = self.sem(inc)
            inc = self.canon(inc)
            self.ctr[inc] += n
            v = self.ctr[inc]

            def run(h, fn=fn, s=s, n=n, v=v):
                ins = fn(h).then_inc(s, n)
                if hasattr(h, "note_update"):
                    h.note_update(s, v)
                return ins
            self.emit(eng, run)
            return v
        self.emit(eng, fn)
        return None

    def dma(self, eng, out, in_, inc=None, **kw):
        return self.op(eng, lambda h, o=out, i=in_, kw=kw: h.dma_start(out=o, in_=i, **kw),
                       inc=inc, n=16)

    def dyn(self, eng, apfn, inc, n=16):
        s = self.sem(inc)
        inc = self.canon(inc)
        self.ctr[inc] += n
        v = self.ctr[inc]
        rg = self.regs

        def run(h, apfn=apfn, s=s, n=n, v=v, rg=rg):
            ins = apfn(h, rg).then_inc(s, n)
            if hasattr(h, "note_update"):
                h.note_update(s, v)
            return ins
        self.emit(eng, run)
        return v

    def wait(self, eng, name, val=None):
        s = self.sem(name)
        v = self.ctr[self.canon(name)] if val is None else val
        self.emit(eng, lambda h, s=s, v=v: h.wait_ge(s, v))

    def val(self, name):
        self.sem(name)
        return self.ctr[self.canon(name)]

    def barrier(self, name, engines=None):
        engs = engines or self.ENGINES
        s = self.sem(name)
        name = self.canon(name)
        tgt = self.ctr[name] + len(engs)
        self.ctr[name] = tgt
        for e in engs:
            self.emit(e, lambda h, s=s: h.sem_inc(s, 1))
        for e in engs:
            self.emit(e, lambda h, s=s, t=tgt: h.wait_ge(s, t))

    def reg(self, eng, name, loader):
        key = (eng, name)
        if key not in self.regs:
            holder = {}
            self.regs[key] = holder
            self.emit(eng, lambda h, hd=holder, ld=loader: hd.__setitem__('v', ld(h)))
        return self.regs[key]

    class DepProxy:

        def __init__(self, h, sem):
            self._h = h
            self._sem = sem
            self._cnt = 0
            self._pend = []
            self._noted = None

        def note_update(self, s, v):
            self._noted = (s, v)
            if self._pend and self._pend[-1][2] is None and self._pend[-1][0] is not None:
                ins, names, _ = self._pend[-1]
                self._pend[-1] = (ins, names, (s, v))

        def _flush_all(self):
            for ins, names, noted in self._pend:
                if noted is not None:
                    self._h.wait_ge(noted[0], noted[1])
                else:
                    self._cnt += 1
                    ins.then_inc(self._sem, 1)
                    self._h.wait_ge(self._sem, self._cnt)
            self._pend = []

        def __getattr__(self, name):
            attr = getattr(self._h, name)
            if not callable(attr) or name.startswith("_"):
                return attr
            emitting = name not in ("wait_ge", "sem_inc", "nop", "note_update",
                                    "value_load")

            def wrapped(*a, _attr=attr, _em=emitting, _nm=name, **k):
                if _em and self._pend:
                    tn = set()
                    for ap in list(a) + list(k.values()):
                        t = getattr(ap, "tensor", None)
                        if t is not None:
                            tn.add(t.name)
                    if any(tn & names for _, names, _n in self._pend):
                        self._flush_all()
                self._noted = None
                ins = _attr(*a, **k)
                if _em:
                    tn = set()
                    for ap in list(a) + list(k.values()):
                        t = getattr(ap, "tensor", None)
                        if t is not None:
                            tn.add(t.name)
                    if tn:
                        self._pend.append((ins, tn, None))
                        if len(self._pend) > 24:
                            self._flush_all()
                return ins
            return wrapped

    def finalize(self):
        nc = self.nc
        vsync = self.sem("vsync")
        ssync = self.sem("ssync")
        gsync = self.sem("gsync")
        with nc.Block() as block:
            @block.tensor
            def _(t):
                for f in self.q["tensor"]:
                    f(t)

            @block.vector
            def _(v):
                vp = Prog.DepProxy(v, vsync)
                for f in self.q["vector"]:
                    f(vp)

            @block.scalar
            def _(s):
                sp = Prog.DepProxy(s, ssync)
                for f in self.q["scalar"]:
                    f(sp)

            @block.gpsimd
            def _(g):
                gp = Prog.DepProxy(g, gsync)
                for f in self.q["gpsimd"]:
                    f(gp)

            @block.sync
            def _(s):
                for f in self.q["sync"]:
                    f(s)


def build_program(c: CFG, nstages=3, tap=None):
    nc = bass.Bass("TRN2", target_bir_lowering=False, debug=False,
                   detect_race_conditions=False)
    p = Prog(nc)
    es = nc.ctx

    def din(name, shape, dt=F32):
        return nc.dram_tensor(name, shape, dt, kind="ExternalInput").ap()

    def dint(name, shape, dt):
        return nc.dram_tensor(name, shape, dt).ap()

    def sb(name, shape, dt):
        return es.enter_context(nc.sbuf_tensor(name, shape, dt))

    X = dict(nc=nc, p=p, c=c)
    X['x0'] = din("x0", [c.HALF, c.D])
    X['qkvw_t'] = din("qkvw_t", [c.D, 3 * c.D])
    X['projw_t'] = din("projw_t", [c.D, c.D])
    X['routerw_t'] = din("routerw_t", [c.D, c.E])
    X['wg_t'] = din("wg_t", [c.E, c.D, c.HFF])
    X['wu_t'] = din("wu_t", [c.E, c.D, c.HFF])
    X['wd_t'] = din("wd_t", [c.E, c.HFF, c.D])
    X['n1w'] = din("n1w", [1, c.D])
    X['n2w'] = din("n2w", [1, c.D])
    X['mlng'] = din("mlng", [1, c.D])
    X['mlnb'] = din("mlnb", [1, c.D])
    X['winm'] = din("winm", [c.D, c.DI])
    X['convw'] = din("convw", [c.DIm, 4])
    X['convb'] = din("convb", [c.DIm, 1])
    X['wdtm'] = din("wdtm", [c.DIm, 1])
    X['bdtm'] = din("bdtm", [c.DIm, 1])
    X['alogm'] = din("alogm", [c.DIm, c.DSt])
    X['dvecm'] = din("dvecm", [c.DIm, 1])
    X['wxpm'] = din("wxpm", [c.DIm, 33])
    X['wout'] = din("wout", [c.DI, c.D])
    X['cosr'] = din("cosr", [c.T, 512])
    X['sinr'] = din("sinr", [c.T, 512])
    X['ident'] = din("ident", [128, 128])
    X['tri'] = din("tri", [128, 128])
    X['selv'] = din("selv", [8, 1], I32)
    X['kvones'] = din("kvones", [c.T, 1])
    X['ohot'] = din("ohot", [33, 33 * 128])
    X['xout'] = nc.dram_tensor("xout", [c.HALF, c.D], F32, kind="ExternalOutput").ap()

    X['kqr'] = dint("kqr", [c.KBUF, c.KQRC], BF16)
    X['vstg'] = dint("vstg", [c.KBUF // 128, 128, c.VSTC], BF16)
    for i, nslot in (("1", 9), ("2", 8), ("5", 9)):
        X['agc' + i] = dint("agc" + i, [c.D, c.HALF], BF16)
        X['ago' + i] = dint("ago" + i, [nslot, c.D, c.HALF], BF16)
    X['agc3'] = dint("agc3", [33, c.T], F32)
    X['ago3'] = dint("ago3", [8, 33, c.T], F32)
    X['agc4'] = dint("agc4", [c.DIm, c.T], BF16)
    X['ago4'] = dint("ago4", [8, c.DIm, c.T], BF16)
    X['y0s'] = dint("y0s", [c.HALF, c.D], F32)
    X['t1s'] = dint("t1s", [c.HALF, c.D], F32)
    X['x1s'] = dint("x1s", [c.HALF, c.D], F32)
    X['mus'] = dint("mus", [c.HALF, c.D], F32)
    X['x2s'] = dint("x2s", [c.HALF, c.D], F32)
    X['xcs'] = dint("xcs", [c.DIm, c.T], BF16)
    X['zss'] = dint("zss", [c.DIm, c.T], BF16)
    X['xnmoe'] = dint("xnmoe", [c.D, c.HALF], BF16)
    X['gfull'] = dint("gfull", [c.DI, c.HALF], BF16)
    X['kwin'] = dint("kwin", [c.T, c.KQRC], BF16)
    X['vwin'] = dint("vwin", [c.T // 128, 128, c.VSTC], BF16)
    X['qkvwb'] = dint("qkvwb", [c.D, 3 * c.D], BF16)
    X['projwb'] = dint("projwb", [c.D, c.D], BF16)
    X['wgb'] = dint("wgb", [c.E, c.D, c.HFF], BF16)
    X['wub'] = dint("wub", [c.E, c.D, c.HFF], BF16)
    X['wdb'] = dint("wdb", [c.E, c.HFF, c.D], BF16)
    X['winb'] = dint("winb", [c.D, c.DI], BF16)
    X['woutb'] = dint("woutb", [c.DI, c.D], BF16)
    X['wxpb'] = dint("wxpb", [c.DIm, 33], BF16)

    X['ps'] = [es.enter_context(nc.psum_tensor(f"ps{i}", [128, 512], F32))
               for i in range(8)]
    X['identt'] = sb("identt", [128, 128], F32)
    X['trit'] = sb("trit", [128, 128], F32)
    X['selt'] = sb("selt", [8, 1], I32)
    X['onesf'] = sb("onesf", [128, 1], F32)
    X['onesr'] = sb("onesr", [1, 128], F32)
    X['n1c'] = sb("n1c", [128, c.DT], F32)
    X['n2c'] = sb("n2c", [128, c.DT], F32)
    X['mgc'] = sb("mgc", [128, c.DT], F32)
    X['mbc'] = sb("mbc", [128, c.DT], F32)
    X['m_cw'] = sb("m_cw", [128, c.DIT * 4], F32)
    X['m_cb'] = sb("m_cb", [128, c.DIT], F32)
    X['m_wdt'] = sb("m_wdt", [128, c.DIT], F32)
    X['m_bdt'] = sb("m_bdt", [128, c.DIT], F32)
    X['m_A'] = sb("m_A", [128, c.DIT * c.DSt], F32)
    X['m_D'] = sb("m_D", [128, c.DIT], F32)
    X['ohott'] = sb("ohott", [33, 33 * 128], F32)
    X['negkm'] = sb("negkm", [16, 1], F32)
    X['negkmr'] = sb("negkmr", [1, 16], F32)

    build_setup(X)
    p.barrier("bar_setup")

    def do_tap(name):
        if tap != name:
            return False
        src_t = {'y0': X['y0s'], 't1': X['t1s'], 'x1': X['x1s'], 'mu': X['mus'],
                 'x2': X['x2s']}[name]
        p.barrier("bar_tap")
        p.dma("gpsimd", X['xout'][:, :], src_t[:, :], inc="taps")
        p.wait("gpsimd", "taps", p.val("taps"))
        p.finalize()
        return True

    build_ffn(X, 1, X['x0'], X['x1s'] if nstages > 1 else X['xout'],
              X['agc1'], X['ago1'])
    if tap in ('y0', 't1', 'x1'):
        if do_tap(tap):
            return nc
    if nstages > 1:
        p.barrier("bar_f1")
        build_mamba(X)
        if tap in ('mu', 'x2'):
            if do_tap(tap):
                return nc
    if nstages > 2:
        p.barrier("bar_m")
        build_ffn(X, 2, X['x2s'], X['xout'], X['agc5'], X['ago5'])
    p.finalize()
    return nc


def build_setup(X):
    p, c = X['p'], X['c']
    g = "gpsimd"
    p.dma(g, X['selt'][:, :], X['selv'][:, :], inc="ld0")
    p.dma(g, X['ohott'][:, :], X['ohot'][:, :], inc="ld0")
    p.dma(g, X['identt'][:], X['ident'][:, :], inc="ld0")
    p.dma(g, X['trit'][:], X['tri'][:, :], inc="ld0")
    p.dma(g, S3(X['m_cw'][:, :], 4), T3(X['convw']), inc="ld0")
    p.dma(g, S3(X['m_cb'][:, :], 1), T3(X['convb']), inc="ld0", allow_slow_non_contiguous=True)
    p.dma(g, S3(X['m_wdt'][:, :], 1), T3(X['wdtm']), inc="ld0", allow_slow_non_contiguous=True)
    p.dma(g, S3(X['m_bdt'][:, :], 1), T3(X['bdtm']), inc="ld0", allow_slow_non_contiguous=True)
    p.dma(g, S3(X['m_D'][:, :], 1), T3(X['dvecm']), inc="ld0", allow_slow_non_contiguous=True)
    p.dma(g, S3(X['m_A'][:, :], X['c'].DSt if False else 16), T3(X['alogm']), inc="ld0")
    p.dma(g, X['n1c'][:, :], T3(X['n1w'][0, :].unsqueeze(1)).squeeze(2), inc="ld0", allow_slow_non_contiguous=True)
    p.dma(g, X['n2c'][:, :], T3(X['n2w'][0, :].unsqueeze(1)).squeeze(2), inc="ld0", allow_slow_non_contiguous=True)
    p.dma(g, X['mgc'][:, :], T3(X['mlng'][0, :].unsqueeze(1)).squeeze(2), inc="ld0", allow_slow_non_contiguous=True)
    p.dma(g, X['mbc'][:, :], T3(X['mlnb'][0, :].unsqueeze(1)).squeeze(2), inc="ld0", allow_slow_non_contiguous=True)
    v0 = p.val("ld0")
    p.wait("scalar", "ld0", v0)
    p.op("scalar", lambda h: h.activation(X['m_A'][:, :], X['m_A'][:, :], AF.Exp),
         inc="c0")
    p.wait("vector", "c0")
    p.op("vector", lambda h: h.tensor_scalar_mul(X['m_A'][:, :], X['m_A'][:, :], -1.0),
         inc="c0")
    p.op("vector", lambda h: h.memset(X['onesf'][:], 1.0), inc="c0")
    p.op("vector", lambda h: h.memset(X['onesr'][:], 1.0), inc="c0")

    for eng in ("sync", "scalar", "gpsimd"):
        p.wait(eng, "ld0", v0)
        for nm in ("obb", "slo", "shi", "p0", "p1", "orow", "orowh"):
            p.reg(eng, nm, lambda h, r=SELROW[nm]:
                  h.value_load(X['selt'][r:r + 1, 0:1]))

    p.dma(g, X['qkvwb'][:, :], X['qkvw_t'][:, :], inc="wc")
    for e in range(c.E):
        for hq in range(c.NHQ):
            sl = slice(hq * c.HQ, (hq + 1) * c.HQ)
            p.dma(g, X['wgb'][e, :, sl], X['wg_t'][e, :, sl], inc="wc")
            p.dma(g, X['wub'][e, :, sl], X['wu_t'][e, :, sl], inc="wc")
            p.dma(g, X['wdb'][e, sl, :], X['wd_t'][e, sl, :], inc="wc")
    p.dma(g, X['projwb'][:, :], X['projw_t'][:, :], inc="wc")
    p.dma(g, X['winb'][:, :], X['winm'][:, :], inc="wc")
    p.dma(g, X['woutb'][:, :], X['wout'][:, :], inc="wc")
    p.dma(g, X['wxpb'][:, :], X['wxpm'][:, :], inc="wc")
    X['WC_QKV'] = 16
    X['WC_MOEQ'] = lambda e, hq: 16 + (e * c.NHQ + hq + 1) * 48
    X['WC_PROJ'] = 16 + c.E * c.NHQ * 48 + 16
    X['WC_ALL'] = p.val("wc")

    zt = X['zt'] = X['nc'].ctx.enter_context(
        X['nc'].sbuf_tensor("zt", [128, 1040], BF16))
    p.op("vector", lambda h: h.memset(zt[:], 0.0), inc="c0")
    zv = p.val("c0")
    p.wait("sync", "c0", zv)
    nzr = c.HALF // 128
    for i in range(nzr):
        for j in range(0, c.KQRC, 1040):
            w = min(1040, c.KQRC - j)
            p.dma("sync", X['kqr'][i * 128:(i + 1) * 128, j:j + w], zt[:, 0:w], inc="zs")
        p.dma("sync", X['vstg'][i, :, 0:c.VSTC], zt[:, 0:c.VSTC], inc="zs")
    for i in range(c.DT):
        for j in range(0, c.HALF, 1040):
            w = min(1040, c.HALF - j)
            p.dma("sync", X['ago1'][8, i * 128:(i + 1) * 128, j:j + w], zt[:, 0:w],
                  inc="zs")
            p.dma("sync", X['ago5'][8, i * 128:(i + 1) * 128, j:j + w], zt[:, 0:w],
                  inc="zs")
    p.wait("sync", "zs", p.val("zs"))


def norm_chunk(X, P, xtiles, out_tiles, kind, mvbuf, stbuf, sem):
    p, c = X['p'], X['c']
    n = len(xtiles)
    for i, xt in enumerate(xtiles):
        for s in range(c.NST):
            p.op("vector", lambda h, xt=xt, s=s: h.bn_stats(
                stbuf[:, s * 6:(s + 1) * 6], xt[:, s * 512:(s + 1) * 512]))
        p.op("vector", lambda h, i=i: h.bn_aggr(
            mvbuf[:, 2 * i:2 * i + 2], stbuf[:, 0:6 * c.NST]))
        if kind == "rms":
            p.op("vector", lambda h, i=i: h.scalar_tensor_tensor(
                mvbuf[:, 2 * i + 1:2 * i + 2], mvbuf[:, 2 * i:2 * i + 1],
                mvbuf[:, 2 * i:2 * i + 1], mvbuf[:, 2 * i + 1:2 * i + 2],
                A.mult, A.add))
    eps = EPS_LN if kind == "ln" else EPS_RMS
    for i in range(n):
        p.op("vector", lambda h, i=i: h.tensor_scalar_add(
            mvbuf[:, 2 * i + 1:2 * i + 2], mvbuf[:, 2 * i + 1:2 * i + 2], eps))
        p.op("vector", lambda h, i=i: h.reciprocal(
            mvbuf[:, 2 * i + 1:2 * i + 2], mvbuf[:, 2 * i + 1:2 * i + 2]),
            inc=(f"{P}nv" if i == n - 1 else None))
    p.wait("scalar", f"{P}nv")
    for i in range(n):
        p.op("scalar", lambda h, i=i: h.activation(
            mvbuf[:, 2 * i + 1:2 * i + 2], mvbuf[:, 2 * i + 1:2 * i + 2], AF.Sqrt),
            inc=(f"{P}ns" if i == n - 1 else None))
    p.wait("vector", f"{P}ns")
    for i, (xt, ot) in enumerate(zip(xtiles, out_tiles)):
        last = (i == n - 1)
        if kind == "ln":
            p.op("vector", lambda h, xt=xt, ot=ot, i=i: h.tensor_scalar(
                ot, xt, mvbuf[:, 2 * i:2 * i + 1], mvbuf[:, 2 * i + 1:2 * i + 2],
                A.subtract, A.mult), inc=(sem if last else None))
        else:
            p.op("vector", lambda h, xt=xt, ot=ot, i=i: h.tensor_scalar_mul(
                ot, xt, mvbuf[:, 2 * i + 1:2 * i + 2]), inc=(sem if last else None))


def transpose_tile(X, P, src_tile, dst_sb, dst_col0, gcol, bcol, dst32_sb=None,
                   mm_sem="tmm", ev_sem="tev", pbanks=(6, 7)):
    p, c, ps = X['p'], X['c'], X['ps']
    nb = len(pbanks)
    evdone = []
    for d in range(c.DT):
        bank = ps[pbanks[d % nb]]
        if d >= nb:
            p.wait("tensor", P + ev_sem, evdone[d - nb])
        p.op("tensor", lambda h, d=d, b=bank, s=src_tile: h.transpose(
            b[:, 0:128], s[:, d * 128:(d + 1) * 128], X['identt'][:]), inc=P + mm_sem)
        mv = p.val(P + mm_sem)
        p.wait("scalar", P + mm_sem, mv)
        co = dst_col0 + d * 128
        gc = gcol[:, d:d + 1] if gcol is not None else 1.0
        p.op("scalar", lambda h, b=bank, co=co, gc=gc: h.activation(
            dst_sb[:, co:co + 128], b[:, 0:128], AF.Copy, scale=gc),
            inc=P + ev_sem)
        if bcol is not None:
            bc = bcol[:, d:d + 1]
            p.wait("vector", P + ev_sem)
            p.op("vector", lambda h, co=co, bc=bc: h.tensor_scalar_add(
                dst_sb[:, co:co + 128], dst_sb[:, co:co + 128], bc), inc=P + ev_sem)
        if dst32_sb is not None:
            p.op("scalar", lambda h, b=bank, co=co, gc=gc: h.activation(
                dst32_sb[:, co:co + 128], b[:, 0:128], AF.Copy, scale=gc),
                inc=P + ev_sem)
        evdone.append(p.val(P + ev_sem))


def build_ffn(X, phase, x_in, x_out, agc, ago):
    p, c, nc, ps = X['p'], X['c'], X['nc'], X['ps']
    P = f"f{phase}"

    with ExitStack() as outer:
        def sbo(name, shape, dt):
            return outer.enter_context(nc.sbuf_tensor(f"{P}{name}", shape, dt))
        negkm_o = X['negkm']
        stkCP = X[f'{P}stkCP'] = ExitStack()
        Osb_o = stkCP.enter_context(nc.sbuf_tensor(f"{P}Osb", [128, c.NT * c.D], BF16))
        stkA = ExitStack()

        def sbt(name, shape, dt):
            return stkA.enter_context(nc.sbuf_tensor(f"{P}{name}", shape, dt))

        xa = sbt("xa", [128, 2 * c.D], F32)
        y0t = sbt("y0t", [128, 2 * c.D], F32)
        stb = sbt("stb", [128, 6 * c.NST], F32)
        mvb = sbt("mvb", [128, 4], F32)
        xnT = sbt("xnT", [128, 2 * c.D], BF16)
        bAtp = p.val(f"{P}Atp"); bAst = p.val(f"{P}Ast"); bAsc = p.val(f"{P}Asc")
        for t in range(c.NT):
            bi = t % 2
            xv = xa[:, bi * c.D:(bi + 1) * c.D]
            yv = y0t[:, bi * c.D:(bi + 1) * c.D]
            ov = xnT[:, bi * c.D:(bi + 1) * c.D]
            if t >= 2:
                p.wait("sync", f"{P}Atp", bAtp + (t - 1) * c.DT)
            p.dma("sync", xv, x_in[t * 128:(t + 1) * 128, :], inc=f"{P}Alx")
            p.wait("vector", f"{P}Alx")
            if t >= 2:
                p.wait("vector", f"{P}Ast", bAst + (t - 1) * 16)
            norm_chunk(X, f"{P}A", [xv], [yv], "ln", mvb, stb, f"{P}Ay0")
            p.wait("sync", f"{P}Ay0")
            p.dma("sync", X['y0s'][t * 128:(t + 1) * 128, :], yv, inc=f"{P}Ast")
            norm_chunk(X, f"{P}A", [yv], [xv], "rms", mvb, stb, f"{P}Axn")
            p.wait("tensor", f"{P}Axn")
            if t >= 2:
                p.wait("scalar", f"{P}Asc", bAsc + (t - 1) * 16)
            transpose_tile(X, f"{P}A", xv, xnT, bi * c.D, X['n1c'], None,
                           mm_sem="tp", ev_sem="ev", pbanks=(6, 7))
            p.wait("sync", f"{P}Aev")
            p.dma("sync",
                  T3(agc[:, t * 128:(t + 1) * 128]), S3(ov, 128), inc=f"{P}Asc")
        p.wait("gpsimd", f"{P}Asc", p.val(f"{P}Asc"))
        p.op("gpsimd", lambda h: h.collective_compute(
            "AllGather", A.bypass, replica_groups=[list(range(8))],
            ins=[agc.opt()], outs=[ago[0:8].opt()]), inc=f"{P}ag")
        agv = p.val(f"{P}ag")
        p.barrier(f"{P}barA")
        stkA.close()
        stkB = ExitStack()

        def sbt(name, shape, dt):
            return stkB.enter_context(nc.sbuf_tensor(f"{P}{name}", shape, dt))

        qkvw_sb = sbt("qkvw", [128, c.DT * 3 * c.D], BF16)
        p.wait("sync", "wc", X['WC_QKV'])
        p.dma("sync", S3(qkvw_sb[:, :], 3 * c.D), T3(X['qkvwb']), inc=f"{P}Bw")
        xkv = sbt("xkv", [128, 2 * c.DT * 512], BF16)
        cs = sbt("cs", [128, 2 * 1024], F32)
        qst = sbt("qst", [128, 2 * 2048], BF16)
        kst = sbt("kst", [128, 2 * 2048], BF16)
        vst = sbt("vst", [128, 2 * 1040], BF16)
        rtmp = sbt("rtmp", [128, 1024], F32)
        nrm = sbt("nrm", [128, 32], F32)
        ones16 = sbt("ones16", [128, 16], F32)
        kvo = sbt("kvo", [128, 2], F32)
        kmaxP = sbt("kmaxP", [128, 16], F32)
        negkm = negkm_o
        negkmr = X['negkmr']
        p.op("vector", lambda h: h.memset(kmaxP[:], 0.0))
        p.op("vector", lambda h: h.memset(qst[:, :], 0.0))
        p.op("vector", lambda h: h.memset(kst[:, :], 0.0))
        p.op("vector", lambda h: h.memset(ones16[:, :], 1.0), inc=f"{P}Bini")
        p.wait("gpsimd", f"{P}ag", agv)
        p.wait("gpsimd", f"{P}Bw", p.val(f"{P}Bw"))
        p.wait("tensor", f"{P}Bw", p.val(f"{P}Bw"))
        p.wait("tensor", f"{P}Bini", p.val(f"{P}Bini"))
        bBrp = p.val(f"{P}Brp"); bBve = p.val(f"{P}Bve"); bBst = p.val(f"{P}Bst")
        bBkcd = p.val(f"{P}Bkcd")
        for kc in range(c.NKC):
            ci = kc % 2
            xv = xkv[:, ci * c.DT * 512:(ci + 1) * c.DT * 512]
            slot = 'slo' if kc < c.NKC // 2 else 'shi'
            col0 = (kc % (c.NKC // 2)) * 512
            if kc >= 2:
                p.wait("gpsimd", f"{P}Bkcd", bBkcd + kc - 1)

            def ld(h, rg, slot=slot, col0=col0, xv=xv, ago=ago):
                ap = ago[DS_(rg[("gpsimd", slot)]['v'], 1), :, col0:col0 + 512]
                return h.dma_start(out=S3(xv, 512), in_=T3(ap.squeeze(0)))
            p.dyn("gpsimd", ld, inc=f"{P}Bglx")
            lxv = p.val(f"{P}Bglx")
            for tt in range(4):
                gti = kc * 4 + tt
                ti = gti % 2
                tok0 = kc * 512 + tt * 128
                p.dma("sync", cs[:, ti * 1024:ti * 1024 + 512],
                      X['cosr'][tok0:tok0 + 128, :], inc=f"{P}Bcs")
                p.dma("sync", cs[:, ti * 1024 + 512:ti * 1024 + 1024],
                      X['sinr'][tok0:tok0 + 128, :], inc=f"{P}Bcs")
                p.dma("sync", kvo[:, ti:ti + 1], X['kvones'][tok0:tok0 + 128, :],
                      inc=f"{P}Bcs")
                csv = p.val(f"{P}Bcs")
                p.wait("tensor", f"{P}Bglx", lxv)
                if gti >= 1:
                    p.wait("tensor", f"{P}Brp", bBrp + gti * 5)
                    p.wait("tensor", f"{P}Bve", bBve + gti * 2)
                for nch in range(6):
                    bank = ps[nch]
                    for dt_ in range(c.DT):
                        p.op("tensor",
                             lambda h, b=bank, nch=nch, dt_=dt_, xv=xv, tt=tt:
                             h.matmul(b[:, 0:512],
                                      xv[:, dt_ * 512 + tt * 128:
                                         dt_ * 512 + tt * 128 + 128],
                                      qkvw_sb[:, dt_ * 3 * c.D + nch * 512:
                                              dt_ * 3 * c.D + (nch + 1) * 512],
                                      start=(dt_ == 0), stop=(dt_ == c.DT - 1)),
                             inc=(f"{P}Bsm" if dt_ == c.DT - 1 else None))
                if tt == 3:
                    p.op("tensor", lambda h: h.nop(), inc=f"{P}Bkcd")
                smv = p.val(f"{P}Bsm")
                if gti >= 2:
                    p.wait("vector", f"{P}Bst", bBst + (gti - 1) * 48)
                    p.wait("scalar", f"{P}Bst", bBst + (gti - 1) * 48)
                p.wait("vector", f"{P}Bsm", smv)
                p.wait("vector", f"{P}Bcs", csv)
                cosv = cs[:, ti * 1024:ti * 1024 + 512]
                sinv = cs[:, ti * 1024 + 512:ti * 1024 + 1024]
                for part, stg in ((0, qst), (1, kst)):
                    stgv = stg[:, ti * 2048:(ti + 1) * 2048]
                    for sub in range(2):
                        bank = ps[2 * part + sub]
                        sgv = hv(stgv, 128, 0, 64)[:, sub * 8:sub * 8 + 8, :]
                        p.op("vector", lambda h, b=bank, cosv=cosv: h.tensor_tensor(
                            rtmp[:, 0:512], b[:, 0:512], cosv, A.mult))
                        p.op("vector", lambda h, b=bank, sinv=sinv: h.tensor_tensor(
                            rtmp[:, 512:768],
                            hv(b[:, 0:512], 64, 32, 64),
                            hv(sinv, 64, 0, 32), A.mult))
                        p.op("vector", lambda h, b=bank, sinv=sinv: h.tensor_tensor(
                            rtmp[:, 768:1024],
                            hv(b[:, 0:512], 64, 0, 32),
                            hv(sinv, 64, 32, 64), A.mult))
                        p.op("vector", lambda h, sgv=sgv: h.tensor_tensor(
                            sgv[:, :, 0:32],
                            hv(rtmp[:, 0:512], 64, 0, 32),
                            rtmp[:, 512:768].rearrange("q (hh x) -> q hh x", x=32),
                            A.subtract))
                        p.op("vector", lambda h, sgv=sgv: h.tensor_tensor(
                            sgv[:, :, 32:64],
                            hv(rtmp[:, 0:512], 64, 32, 64),
                            rtmp[:, 768:1024].rearrange("q (hh x) -> q hh x", x=32),
                            A.add))
                    p.op("vector", lambda h, stgv=stgv: h.tensor_tensor(
                        rtmp[:, 0:1024],
                        hv(stgv, 128, 0, 64, nh=16),
                        hv(stgv, 128, 0, 64, nh=16), A.mult))
                    p.op("vector", lambda h, part=part: h.reduce_sum(
                        nrm[:, part * 16:part * 16 + 16],
                        rtmp[:, 0:1024].rearrange("q (hh x) -> q hh x", x=64),
                        axis=AX),
                        inc=(f"{P}Bnr" if part == 1 else None))
                nv = p.val(f"{P}Bnr")
                p.wait("scalar", f"{P}Bnr", nv)
                p.op("scalar", lambda h: h.activation(
                    nrm[:, 0:16], nrm[:, 0:16], AF.Sqrt), inc=f"{P}Bqn")
                qnv = p.val(f"{P}Bqn")
                p.wait("vector", f"{P}Bqn", qnv)
                qstv = qst[:, ti * 2048:(ti + 1) * 2048]
                p.op("vector", lambda h, qstv=qstv: h.tensor_copy(
                    hv(qstv, 128, 64, 65, nh=16), nrm[:, 0:16].rearrange(
                        "q (hh x) -> q hh x", x=1)))
                p.op("vector", lambda h: h.tensor_max(
                    kmaxP[:, 0:16], kmaxP[:, 0:16], nrm[:, 16:32]))
                kstv_ = kst[:, ti * 2048:(ti + 1) * 2048]
                p.op("vector", lambda h, kstv_=kstv_, ti=ti: h.tensor_scalar_mul(
                    hv(kstv_, 128, 64, 65, nh=16),
                    ones16[:, :].rearrange("q (hh x) -> q hh x", x=1),
                    kvo[:, ti:ti + 1]), inc=f"{P}Brp", n=5)
                p.wait("scalar", f"{P}Bsm", smv)
                vstv = vst[:, ti * 1040:(ti + 1) * 1040]
                for sub in range(2):
                    bank = ps[4 + sub]
                    p.op("scalar", lambda h, b=bank, sub=sub, vstv=vstv: h.copy(
                        hv(vstv, 65, 0, 64)[:, sub * 8:sub * 8 + 8, :].rearrange(
                            "q hh x -> q hh x") if False else
                        hv(vstv, 65, 0, 64, nh=16)[:, sub * 8:sub * 8 + 8, :],
                        b[:, 0:512].rearrange("q (hh x) -> q hh x", x=64)),
                        inc=f"{P}Bve")
                vev = p.val(f"{P}Bve")
                vstv2 = vst[:, ti * 1040:(ti + 1) * 1040]
                p.op("vector", lambda h, vstv2=vstv2, ti=ti: h.tensor_scalar_mul(
                    hv(vstv2, 65, 64, 65, nh=16),
                    ones16[:, :].rearrange("q (hh x) -> q hh x", x=1),
                    kvo[:, ti:ti + 1]), inc=f"{P}Bvo")
                p.wait("sync", f"{P}Brp", p.val(f"{P}Brp"))
                p.wait("sync", f"{P}Bvo", p.val(f"{P}Bvo"))
                p.wait("sync", f"{P}Bve", vev)
                row0 = c.HALF + tok0
                p.dma("sync", X['kqr'][row0:row0 + 128, 0:2048], qstv, inc=f"{P}Bst")
                p.dma("sync", X['kqr'][row0:row0 + 128, 2048:4096],
                      kst[:, ti * 2048:(ti + 1) * 2048], inc=f"{P}Bst")
                p.dma("sync", X['vstg'][row0 // 128, :, :], vstv, inc=f"{P}Bst")
        p.wait("tensor", f"{P}Brp", p.val(f"{P}Brp"))
        p.op("tensor", lambda h: h.transpose(
            ps[6][0:16, 0:128], kmaxP[:, 0:16], X['identt'][:]), inc=f"{P}Bkt")
        p.wait("vector", f"{P}Bkt")
        p.op("vector", lambda h: h.reduce_max(
            negkm[0:16, 0:1], ps[6][0:16, 0:128], axis=AX), inc=f"{P}Bkm")
        p.wait("scalar", f"{P}Bkm", p.val(f"{P}Bkm"))
        p.op("scalar", lambda h: h.activation(
            negkm[0:16, 0:1], negkm[0:16, 0:1], AF.Sqrt), inc=f"{P}Bkm")
        p.wait("vector", f"{P}Bkm", p.val(f"{P}Bkm"))
        p.op("vector", lambda h: h.tensor_scalar_mul(
            negkm[0:16, 0:1], negkm[0:16, 0:1], -1.0), inc=f"{P}Bkm")
        p.wait("tensor", f"{P}Bkm", p.val(f"{P}Bkm"))
        p.op("tensor", lambda h: h.transpose(
            ps[7][0:1, 0:16], negkm[0:16, 0:1], X['identt'][0:16, 0:16]),
            inc=f"{P}Bkm")
        p.wait("vector", f"{P}Bkm", p.val(f"{P}Bkm"))
        p.op("vector", lambda h: h.tensor_copy(negkmr[0:1, 0:16], ps[7][0:1, 0:16]),
             inc=f"{P}Bkm")
        p.wait("sync", f"{P}Bst", p.val(f"{P}Bst"))
        p.barrier(f"{P}barB")
        stkB.close()
        X[f'{P}Osb'] = Osb_o
        with ExitStack() as stkC:
            def sbc(name, shape, dt):
                return stkC.enter_context(nc.sbuf_tensor(f"{P}{name}", shape, dt))
            build_attn(X, phase, P, sbc, negkm)
        p.barrier(f"{P}barC")
        build_proj_moe(X, phase, P, None, x_in, x_out, agc, ago)


def build_attn(X, phase, P, sbt, negkm):
    p, c, nc, ps = X['p'], X['c'], X['nc'], X['ps']
    QG, QB = c.QG, c.QB
    ZB = c.HALF // 128
    Kh0 = sbt("Kh0", [128, c.T], BF16)
    Kh1 = sbt("Kh1", [128, c.T], BF16)
    Qh0 = sbt("Qh0", [128, c.HALF], BF16)
    Qh1 = sbt("Qh1", [128, c.HALF], BF16)
    PT = sbt("PT", [128, 2 * QG], BF16)
    KBmax = ZB + c.QB * c.NQG
    Vw0 = sbt("Vw0", [128, KBmax * 65], BF16)
    Vw1 = sbt("Vw1", [128, KBmax * 65], BF16)
    Osb = X[f'{P}Osb']
    rl = sbt("rl", [128, 2], F32)

    p.dyn("sync", lambda h, rg: h.dma_start(
        out=X['kwin'][:, :],
        in_=X['kqr'][DS_(rg[("sync", "orow")]['v'], c.T), :]), inc=f"{P}Cwin")
    p.dyn("sync", lambda h, rg: h.dma_start(
        out=X['vwin'][:, :, :],
        in_=X['vstg'][DS_(rg[("sync", "obb")]['v'], c.T // 128), :, :]),
        inc=f"{P}Cwin")
    p.wait("sync", f"{P}Cwin", p.val(f"{P}Cwin"))

    def kload(h, rg, hd, dst):
        ap = X['kwin'][0:c.T, c.H * 128 + 128 * hd:c.H * 128 + 128 * hd + 128]
        return h.dma_start(out=dst[:, :], in_=ap, transpose=True)

    def qload(h, rg, hd, dst):
        ap = X['kwin'][c.HALF:2 * c.HALF, 128 * hd:128 * hd + 128]
        return h.dma_start(out=dst[:, :], in_=ap, transpose=True)

    def vload(h, rg, hd, nkb, dst):
        ap = X['vwin'][0:nkb, :, 65 * hd:65 * hd + 65]
        return h.dma_start(out=S3(dst[:, 0:nkb * 65], 65),
                           in_=ap.transpose([1, 0, 2]))

    p.dyn("sync", lambda h, rg: kload(h, rg, 0, Kh0), inc=f"{P}Ckl")
    p.dyn("sync", lambda h, rg: qload(h, rg, 0, Qh0), inc=f"{P}Ckl")
    p.dyn("sync", lambda h, rg: vload(h, rg, 0, KBmax, Vw0), inc=f"{P}Cvl")

    bChd = p.val(f"{P}Chd")
    bCex = p.val(f"{P}Cex")
    bCavk = p.val(f"{P}Cavk")
    for hd in range(c.H):
        Khv = (Kh0, Kh1)[hd % 2]
        Qhv = (Qh0, Qh1)[hd % 2]
        Vwv = (Vw0, Vw1)[hd % 2]
        klv = p.val(f"{P}Ckl")
        vlv = p.val(f"{P}Cvl")
        if hd + 1 < c.H:
            p.wait("sync", f"{P}Chd", bChd + hd)
            p.dyn("sync", lambda h, rg, hd=hd: kload(h, rg, hd + 1, (Kh0, Kh1)[(hd + 1) % 2]),
                  inc=f"{P}Ckl")
            p.dyn("sync", lambda h, rg, hd=hd: qload(h, rg, hd + 1, (Qh0, Qh1)[(hd + 1) % 2]),
                  inc=f"{P}Ckl")
            p.dyn("sync", lambda h, rg, hd=hd: vload(h, rg, hd + 1, KBmax, (Vw0, Vw1)[(hd + 1) % 2]),
                  inc=f"{P}Cvl")
        p.wait("vector", f"{P}Ckl", klv)
        p.op("vector", lambda h, Qhv=Qhv, hd=hd: h.tensor_scalar_mul(
            Qhv[64:65, :], Qhv[64:65, :], X['negkmr'][0:1, hd:hd + 1]),
            inc=f"{P}Cqf")
        qfv = p.val(f"{P}Cqf")
        p.wait("tensor", f"{P}Cqf", qfv)
        p.wait("tensor", f"{P}Cvl", vlv)
        for J in range(c.NQG):
            KB = ZB + QB * (J + 1)
            ex0 = p.val(f"{P}Cex")
            avk0 = p.val(f"{P}Cavk")
            for kb in range(KB):
                sbank = ps[kb % 2]
                if kb >= 2:
                    p.wait("tensor", f"{P}Cex", ex0 + kb - 1)
                elif ex0 > bCex:
                    p.wait("tensor", f"{P}Cex", ex0)
                p.op("tensor", lambda h, b=sbank, kb=kb, J=J, Khv=Khv, Qhv=Qhv:
                     h.matmul(b[:, 0:QG], Khv[0:65, kb * 128:(kb + 1) * 128],
                              Qhv[0:65, J * QG:(J + 1) * QG], start=True, stop=True),
                     inc=f"{P}Csm")
                smv = p.val(f"{P}Csm")
                nbias = 0
                for jb in range(QB):
                    diag = ZB + J * QB + jb
                    if kb >= diag:
                        if nbias == 0:
                            p.wait("vector", f"{P}Csm", smv)
                        nbias += 1
                        if kb == diag:
                            p.op("vector", lambda h, b=sbank, jb=jb: h.tensor_add(
                                b[:, jb * 128:(jb + 1) * 128],
                                b[:, jb * 128:(jb + 1) * 128], X['trit'][:]),
                                inc=f"{P}Cbi")
                        else:
                            p.op("vector", lambda h, b=sbank, jb=jb: h.tensor_scalar_add(
                                b[:, jb * 128:(jb + 1) * 128],
                                b[:, jb * 128:(jb + 1) * 128], NEG), inc=f"{P}Cbi")
                p.wait("scalar", f"{P}Csm", smv)
                if nbias:
                    p.wait("scalar", f"{P}Cbi", p.val(f"{P}Cbi"))
                if kb >= 2:
                    p.wait("scalar", f"{P}Cavk", avk0 + kb - 1)
                elif avk0 > bCavk:
                    p.wait("scalar", f"{P}Cavk", avk0)
                ptv = PT[:, (kb % 2) * QG:(kb % 2 + 1) * QG]
                p.op("scalar", lambda h, b=sbank, ptv=ptv: h.activation(
                    ptv, b[:, 0:QG], AF.Exp, scale=0.125), inc=f"{P}Cex")
                exv = p.val(f"{P}Cex")
                p.wait("tensor", f"{P}Cex", exv)
                for jb in range(QB):
                    p.op("tensor", lambda h, kb=kb, jb=jb, KB=KB, ptv=ptv, Vwv=Vwv:
                         h.matmul(ps[2 + jb][:, 0:65],
                                  ptv[:, jb * 128:(jb + 1) * 128],
                                  Vwv[:, kb * 65:(kb + 1) * 65],
                                  start=(kb == 0), stop=(kb == KB - 1)),
                         inc=(f"{P}Cavk" if jb == QB - 1 else None))
            avv = p.val(f"{P}Cavk")
            p.wait("vector", f"{P}Cavk", avv)
            for jb in range(QB):
                tt = J * QB + jb
                p.op("vector", lambda h, jb=jb: h.reciprocal(
                    rl[:, 0:1], ps[2 + jb][:, 64:65]))
                p.op("vector", lambda h, jb=jb, tt=tt, hd=hd: h.tensor_scalar_mul(
                    Osb[:, tt * c.D + hd * 64:tt * c.D + hd * 64 + 64],
                    ps[2 + jb][:, 0:64], rl[:, 0:1]),
                    inc=(f"{P}Cov" if jb == QB - 1 else None))
            p.wait("tensor", f"{P}Cov", p.val(f"{P}Cov"))
        p.op("tensor", lambda h: h.nop(), inc=f"{P}Chd")
    p.barrier(f"{P}barAt", engines=("tensor", "vector", "scalar", "sync"))


def build_proj_moe(X, phase, P, sbt_unused, x_in, x_out, agc, ago):
    p, c, nc, ps = X['p'], X['c'], X['nc'], X['ps']
    Osb = X[f'{P}Osb']
    stkCP = X[f'{P}stkCP']

    def sbcp(name, shape, dt):
        return stkCP.enter_context(nc.sbuf_tensor(f"{P}{name}", shape, dt))

    pw = sbcp("pw", [128, c.DT * c.D], BF16)
    p.wait("sync", "wc", X['WC_PROJ'])
    p.dma("sync", S3(pw[:, :], c.D), T3(X['projwb']), inc=f"{P}Pw")
    OT = sbcp("OT", [128, 2 * c.D], BF16)
    y0b = sbcp("y0b", [128, 2 * c.D], F32)
    t1b = sbcp("t1b", [128, 2 * c.D], F32)
    p.wait("tensor", f"{P}Pw", p.val(f"{P}Pw"))
    bPmm = p.val(f"{P}Pmm"); bPst = p.val(f"{P}Pst")
    for t in range(c.NT):
        bi = t % 2
        otv = OT[:, bi * c.D:(bi + 1) * c.D]
        y0v = y0b[:, bi * c.D:(bi + 1) * c.D]
        t1v = t1b[:, bi * c.D:(bi + 1) * c.D]
        if t >= 2:
            p.wait("sync", f"{P}Pmm", bPmm + (t - 1) * (c.D // 512))
            p.wait("sync", f"{P}Pst", bPst + (t - 1) * 16)
        for d in range(c.DT):
            p.dma("sync", otv[:, d * 128:(d + 1) * 128],
                  Osb[:, t * c.D + d * 128:t * c.D + (d + 1) * 128], transpose=True,
                  inc=f"{P}Pot")
        p.dma("sync", y0v, X['y0s'][t * 128:(t + 1) * 128, :], inc=f"{P}Pot")
        otc = p.val(f"{P}Pot")
        p.wait("tensor", f"{P}Pot", otc)
        for nch in range(c.D // 512):
            bank = ps[nch % 2]
            for d in range(c.DT):
                p.op("tensor", lambda h, b=bank, nch=nch, d=d, otv=otv: h.matmul(
                    b[:, 0:512], otv[:, d * 128:(d + 1) * 128],
                    pw[:, d * c.D + nch * 512:d * c.D + (nch + 1) * 512],
                    start=(d == 0), stop=(d == c.DT - 1)),
                    inc=(f"{P}Pmm" if d == c.DT - 1 else None))
            mmv = p.val(f"{P}Pmm")
            p.wait("vector", f"{P}Pmm", mmv)
            p.op("vector", lambda h, b=bank, nch=nch, t1v=t1v, y0v=y0v: h.tensor_add(
                t1v[:, nch * 512:(nch + 1) * 512], b[:, 0:512],
                y0v[:, nch * 512:(nch + 1) * 512]),
                inc=(f"{P}Pt1" if nch == c.D // 512 - 1 else None))
        p.wait("sync", f"{P}Pt1", p.val(f"{P}Pt1"))
        p.dma("sync", X['t1s'][t * 128:(t + 1) * 128, :], t1v, inc=f"{P}Pst")
        p.wait("tensor", f"{P}Pt1", p.val(f"{P}Pt1"))
    p.barrier(f"{P}barP", engines=("tensor", "vector", "scalar", "sync"))
    stkCP.close()
    stkD = ExitStack()

    def sbt(name, shape, dt):
        return stkD.enter_context(nc.sbuf_tensor(f"{P}{name}", shape, dt))

    y0b = sbt("y0b2", [128, 2 * c.D], F32)
    t1b = sbt("t1b2", [128, 2 * c.D], F32)
    stb2 = sbt("stb2", [128, 6 * c.NST], F32)
    mvb2 = sbt("mvb2", [128, 8], F32)
    xn2T = sbt("xn2T", [128, c.D], BF16)
    xn232 = sbt("xn232", [128, c.D], F32)
    rw = sbt("rw", [128, c.DT * c.E], F32)
    gts = sbt("gts", [128, c.NT * c.E], F32)
    gsc = sbt("gsc", [128, 20], F32)
    p.dma("sync", S3(rw[:, :], c.E), T3(X['routerw_t']), inc=f"{P}Dw")
    p.wait("tensor", f"{P}Dw", p.val(f"{P}Dw"))
    bDxn = p.val(f"{P}Dxn"); bDxs = p.val(f"{P}Dxs"); bDrt = p.val(f"{P}Drt")
    for t in range(c.NT):
        bi = t % 2
        t1v = t1b[:, bi * c.D:(bi + 1) * c.D]
        if t >= 2:
            p.wait("sync", f"{P}Dxn", bDxn + t - 1)
        p.dma("sync", t1v, X['t1s'][t * 128:(t + 1) * 128, :], inc=f"{P}Dlx")
        p.wait("vector", f"{P}Dlx")
        xnv = y0b[:, bi * c.D:(bi + 1) * c.D]
        norm_chunk(X, f"{P}D", [t1v], [xnv], "rms", mvb2, stb2, f"{P}Dxn")
        p.wait("tensor", f"{P}Dxn")
        if t >= 1:
            p.wait("scalar", f"{P}Dxs", bDxs + t * 16)
            p.wait("scalar", f"{P}Drt", bDrt + t)
        transpose_tile(X, f"{P}D", xnv, xn2T, 0, X['n2c'], None, dst32_sb=xn232,
                       mm_sem="tp", ev_sem="ev", pbanks=(6, 7))
        p.wait("sync", f"{P}Dev")
        p.dma("sync", T3(X['xnmoe'][:, t * 128:(t + 1) * 128]),
              S3(xn2T[:, :], 128), inc=f"{P}Dxs")
        p.wait("tensor", f"{P}Dev", p.val(f"{P}Dev"))
        for d in range(c.DT):
            p.op("tensor", lambda h, d=d: h.matmul(
                ps[4][:, 0:c.E], xn232[:, d * 128:(d + 1) * 128],
                rw[:, d * c.E:(d + 1) * c.E],
                start=(d == 0), stop=(d == c.DT - 1)),
                inc=(f"{P}Drt" if d == c.DT - 1 else None))
        rtv = p.val(f"{P}Drt")
        p.wait("vector", f"{P}Drt", rtv)
        E = c.E
        gv = gts[:, t * E:(t + 1) * E]
        p.op("vector", lambda h, gv=gv: h.tensor_copy(gv, ps[4][:, 0:c.E]))
        p.op("vector", lambda h, gv=gv: h.reduce_max(gsc[:, 0:1], gv, axis=AX))
        p.op("vector", lambda h, gv=gv: h.tensor_scalar(
            gsc[:, 2:2 + E], gv, gsc[:, 0:1], None, A.is_ge))
        p.op("vector", lambda h, gv=gv: h.scalar_tensor_tensor(
            gsc[:, 2:2 + E], gsc[:, 2:2 + E], NEG, gv, A.mult, A.add))
        p.op("vector", lambda h: h.reduce_max(gsc[:, 1:2], gsc[:, 2:2 + E], axis=AX))
        p.op("vector", lambda h: h.tensor_scalar(
            gsc[:, 2:2 + E], gsc[:, 2:2 + E], gsc[:, 1:2], None, A.is_ge))
        p.op("vector", lambda h: h.tensor_tensor(
            gsc[:, 0:1], gsc[:, 1:2], gsc[:, 0:1], A.subtract), inc=f"{P}Dg1")
        p.wait("scalar", f"{P}Dg1", p.val(f"{P}Dg1"))
        p.op("scalar", lambda h: h.activation(gsc[:, 0:1], gsc[:, 0:1], AF.Exp),
             inc=f"{P}Dg2")
        p.wait("vector", f"{P}Dg2", p.val(f"{P}Dg2"))
        p.op("vector", lambda h: h.tensor_scalar_add(gsc[:, 0:1], gsc[:, 0:1], 1.0))
        p.op("vector", lambda h: h.reciprocal(gsc[:, 0:1], gsc[:, 0:1]))
        p.op("vector", lambda h: h.tensor_scalar(
            gsc[:, 1:2], gsc[:, 0:1], -1.0, 1.0, A.mult, A.add))
        p.op("vector", lambda h, gv=gv: h.reduce_max(gsc[:, 18:19], gv, axis=AX))
        p.op("vector", lambda h, gv=gv: h.tensor_scalar(
            gsc[:, 10:10 + E], gv, gsc[:, 18:19], None, A.is_ge))
        p.op("vector", lambda h, gv=gv: h.tensor_scalar_mul(
            gsc[:, 10:10 + E], gsc[:, 10:10 + E], gsc[:, 0:1]))
        p.op("vector", lambda h, gv=gv: h.scalar_tensor_tensor(
            gv, gsc[:, 2:2 + E], gsc[:, 1:2], gsc[:, 10:10 + E], A.mult, A.add),
            inc=f"{P}Dgt")
    p.barrier(f"{P}barD", engines=("tensor", "vector", "scalar", "sync"))
    stkM = ExitStack()

    def sbm(name, shape, dt):
        return stkM.enter_context(nc.sbuf_tensor(f"{P}{name}", shape, dt))

    wgq = sbm("wgq", [128, 2 * c.DT * c.HQ], BF16)
    wuq = sbm("wuq", [128, 2 * c.DT * c.HQ], BF16)
    wdq = sbm("wdq", [128, 2 * c.HQT * c.D], BF16)
    xnm = sbm("xnm", [128, c.DT * c.TG], BF16)
    hpr = sbm("hpr", [128, c.HQT * 512], BF16)
    sg = sbm("sg", [128, c.HQT * 512], BF16)
    yac = sbm("yac", [128, (c.TG // 128) * c.D], F32)
    NCC = c.TG // 512
    bMwdt = p.val(f"{P}Mwdt")
    bMhp = p.val(f"{P}Mhp")
    bMye = p.val(f"{P}Mye")
    for tg in range(c.NTG):
        p.dma("sync", S3(xnm[:, :], c.TG),
              T3(X['xnmoe'][:, tg * c.TG:(tg + 1) * c.TG]), inc=f"{P}Mxl")
        p.op("vector", lambda h: h.memset(yac[:, :], 0.0), inc=f"{P}My0")
        p.wait("tensor", f"{P}Mxl", p.val(f"{P}Mxl"))
        for e in range(c.E):
            for hq in range(c.NHQ):
                gi = (tg * c.E + e) * c.NHQ + hq
                wb = gi % 2
                if phase == 1:
                    p.wait("sync", "wc", X['WC_MOEQ'](e, hq))
                if gi >= 2:
                    p.wait("sync", f"{P}Mwdt", bMwdt + gi - 1)
                wgv = wgq[:, wb * c.DT * c.HQ:(wb + 1) * c.DT * c.HQ]
                wuv = wuq[:, wb * c.DT * c.HQ:(wb + 1) * c.DT * c.HQ]
                wdv = wdq[:, wb * c.HQT * c.D:(wb + 1) * c.HQT * c.D]
                sl = slice(hq * c.HQ, (hq + 1) * c.HQ)
                p.dma("sync", S3(wgv, c.HQ), T3(X['wgb'][e, :, sl]), inc=f"{P}Mwl")
                p.dma("sync", S3(wuv, c.HQ), T3(X['wub'][e, :, sl]), inc=f"{P}Mwl")
                p.dma("sync", S3(wdv, c.D), T3(X['wdb'][e, sl, :]), inc=f"{P}Mwl")
                wlv = p.val(f"{P}Mwl")
                p.wait("tensor", f"{P}Mwl", wlv)
                for cc in range(NCC):
                    for m in range(c.HQT):
                        gb = ps[m % 2]
                        ub = ps[2 + m % 2]
                        wv_ = p.val(f"{P}Mhp") - 1
                        if wv_ > bMhp:
                            p.wait("tensor", f"{P}Mhp", wv_)
                        for d in range(c.DT):
                            p.op("tensor", lambda h, b=gb, m=m, d=d, wgv=wgv, cc=cc:
                                 h.matmul(b[:, 0:512],
                                          wgv[:, d * c.HQ + m * 128:d * c.HQ + m * 128 + 128],
                                          xnm[:, d * c.TG + cc * 512:d * c.TG + (cc + 1) * 512],
                                          start=(d == 0), stop=(d == c.DT - 1)),
                                 inc=(f"{P}Mgm" if d == c.DT - 1 else None))
                        for d in range(c.DT):
                            p.op("tensor", lambda h, b=ub, m=m, d=d, wuv=wuv, cc=cc:
                                 h.matmul(b[:, 0:512],
                                          wuv[:, d * c.HQ + m * 128:d * c.HQ + m * 128 + 128],
                                          xnm[:, d * c.TG + cc * 512:d * c.TG + (cc + 1) * 512],
                                          start=(d == 0), stop=(d == c.DT - 1)),
                                 inc=(f"{P}Mum" if d == c.DT - 1 else None))
                        gmv = p.val(f"{P}Mgm")
                        umv = p.val(f"{P}Mum")
                        p.wait("scalar", f"{P}Mgm", gmv)
                        p.op("scalar", lambda h, b=gb, m=m: h.activation(
                            sg[:, m * 512:(m + 1) * 512], b[:, 0:512], AF.Silu),
                            inc=f"{P}Msl")
                        p.wait("vector", f"{P}Msl", p.val(f"{P}Msl"))
                        p.wait("vector", f"{P}Mum", umv)
                        p.op("vector", lambda h, b=ub, m=m: h.tensor_tensor(
                            hpr[:, m * 512:(m + 1) * 512],
                            sg[:, m * 512:(m + 1) * 512], b[:, 0:512], A.mult),
                            inc=f"{P}Mhp")
                    p.wait("tensor", f"{P}Mhp", p.val(f"{P}Mhp"))
                    for tt in range(4):
                        for dch in range(c.D // 512):
                            yb = ps[4 + 2 * (tt % 2) + dch]
                            wv_ = p.val(f"{P}Mye") - 2
                            if wv_ > bMye:
                                p.wait("tensor", f"{P}Mye", wv_)
                            for m in range(c.HQT):
                                p.op("tensor",
                                     lambda h, b=yb, tt=tt, dch=dch, m=m, wdv=wdv:
                                     h.matmul(b[:, 0:512],
                                              hpr[:, m * 512 + tt * 128:m * 512 + tt * 128 + 128],
                                              wdv[:, m * c.D + dch * 512:m * c.D + (dch + 1) * 512],
                                              start=(m == 0), stop=(m == c.HQT - 1)),
                                     inc=(f"{P}Mym" if m == c.HQT - 1 else None))
                            ymv = p.val(f"{P}Mym")
                            p.wait("vector", f"{P}Mym", ymv)
                            gcol = gts[:, (tg * c.TG // 128 + cc * 4 + tt) * c.E + e:
                                       (tg * c.TG // 128 + cc * 4 + tt) * c.E + e + 1]
                            yv = yac[:, (cc * 4 + tt) * c.D + dch * 512:
                                     (cc * 4 + tt) * c.D + (dch + 1) * 512]
                            p.op("vector", lambda h, b=yb, yv=yv, gcol=gcol:
                                 h.scalar_tensor_tensor(yv, b[:, 0:512], gcol, yv,
                                                        A.mult, A.add), inc=f"{P}Mye")
                p.op("tensor", lambda h: h.nop(), inc=f"{P}Mwdt")
        p.wait("vector", f"{P}Mye", p.val(f"{P}Mye"))
        for tt in range(c.TG // 128):
            t = tg * (c.TG // 128) + tt
            bi = tt % 2
            xv = OT[:, bi * c.D:(bi + 1) * c.D].bitcast(F32) if False else None
            xv = y0b[:, bi * c.D:(bi + 1) * c.D]
            tv = t1b[:, bi * c.D:(bi + 1) * c.D]
            if tt >= 2:
                p.wait("sync", f"{P}Mos", (p.val(f"{P}Mos") - 16))
            p.dma("sync", xv, x_in[t * 128:(t + 1) * 128, :], inc=f"{P}Mol")
            p.dma("sync", tv, X['t1s'][t * 128:(t + 1) * 128, :], inc=f"{P}Mol")
            olv = p.val(f"{P}Mol")
            p.wait("vector", f"{P}Mol", olv)
            p.op("vector", lambda h, xv=xv, tv=tv: h.scalar_tensor_tensor(
                tv, tv, 0.5, xv, A.mult, A.add))
            p.op("vector", lambda h, tv=tv, tt=tt: h.scalar_tensor_tensor(
                tv, yac[:, tt * c.D:(tt + 1) * c.D], 0.5, tv, A.mult, A.add),
                inc=f"{P}Moc")
            p.wait("sync", f"{P}Moc", p.val(f"{P}Moc"))
            p.dma("sync", x_out[t * 128:(t + 1) * 128, :], tv, inc=f"{P}Mos")
    p.wait("sync", f"{P}Mos", p.val(f"{P}Mos"))
    p.barrier(f"{P}barO")
    stkM.close()
    stkD.close()


def build_mamba(X):
    p, c, nc, ps = X['p'], X['c'], X['nc'], X['ps']
    P = "m"
    with ExitStack() as outer:
        stk0 = ExitStack()

        def sbt(name, shape, dt):
            return stk0.enter_context(nc.sbuf_tensor(f"{P}{name}", shape, dt))

        xa = sbt("xa", [128, 2 * c.D], F32)
        ub = sbt("ub", [128, 2 * c.D], F32)
        stb = sbt("stb", [128, 6 * c.NST], F32)
        mvb = sbt("mvb", [128, 4], F32)
        xnT = sbt("xnT", [128, 2 * c.D], BF16)
        bAtp = p.val(f"{P}Atp"); bAst = p.val(f"{P}Ast"); bAsc = p.val(f"{P}Asc")
        for t in range(c.NT):
            bi = t % 2
            xv = xa[:, bi * c.D:(bi + 1) * c.D]
            uv = ub[:, bi * c.D:(bi + 1) * c.D]
            ov = xnT[:, bi * c.D:(bi + 1) * c.D]
            if t >= 2:
                p.wait("sync", f"{P}Atp", bAtp + (t - 1) * c.DT)
                p.wait("vector", f"{P}Ast", bAst + (t - 1) * 16)
            p.dma("sync", xv, X['x1s'][t * 128:(t + 1) * 128, :], inc=f"{P}Alx")
            p.wait("vector", f"{P}Alx")
            norm_chunk(X, f"{P}A", [xv], [uv], "ln", mvb, stb, f"{P}Au")
            p.wait("sync", f"{P}Au")
            p.dma("sync", X['mus'][t * 128:(t + 1) * 128, :], uv, inc=f"{P}Ast")
            norm_chunk(X, f"{P}A", [uv], [xv], "ln", mvb, stb, f"{P}Axn")
            p.wait("tensor", f"{P}Axn")
            if t >= 2:
                p.wait("scalar", f"{P}Asc", bAsc + (t - 1) * 16)
            transpose_tile(X, f"{P}A", xv, xnT, bi * c.D, X['mgc'], X['mbc'],
                           mm_sem="tp", ev_sem="ev", pbanks=(6, 7))
            p.wait("sync", f"{P}Aev")
            p.dma("sync",
                  T3(X['agc2'][:, t * 128:(t + 1) * 128]), S3(ov, 128), inc=f"{P}Asc")
        p.wait("gpsimd", f"{P}Asc", p.val(f"{P}Asc"))
        p.op("gpsimd", lambda h: h.collective_compute(
            "AllGather", A.bypass, replica_groups=[list(range(8))],
            ins=[X['agc2'].opt()], outs=[X['ago2'].opt()]), inc=f"{P}ag2")
        ag2v = p.val(f"{P}ag2")
        p.wait("gpsimd", f"{P}ag2", ag2v)
        p.barrier(f"{P}bar0")
        stk0.close()
        stk1 = ExitStack()

        def sbt(name, shape, dt):
            return stk1.enter_context(nc.sbuf_tensor(f"{P}{name}", shape, dt))

        winb_sb = sbt("win", [128, c.DT * c.DI], BF16)
        wxp_sb = sbt("wxp", [128, c.DIT * 33], BF16)
        p.wait("sync", "wc", X['WC_ALL'])
        p.dma("sync", S3(winb_sb[:, :], c.DI), T3(X['winb']), inc=f"{P}Ww")
        p.dma("sync", S3(wxp_sb[:, :], 33), T3(X['wxpb']), inc=f"{P}Ww")
        xkv = sbt("xkv", [128, 2 * c.DT * 512], BF16)
        xh = sbt("xh", [128, c.DIT * 512], F32)
        halo = sbt("halo", [128, c.DIT * 4], F32)
        cacc = sbt("cacc", [128, 512], F32)
        xcb = sbt("xcb", [128, c.DIT * 512], BF16)
        zsb = sbt("zsb", [128, c.DIT * 512], BF16)
        xp33 = sbt("xp33", [33, 512], F32)
        p.op("vector", lambda h: h.memset(halo[:, :], 0.0), inc=f"{P}hl0")
        p.wait("tensor", f"{P}Ww", p.val(f"{P}Ww"))
        bMk = p.val(f"{P}Mk")
        bMze = p.val(f"{P}Mze")
        bMxe = p.val(f"{P}Mxe")
        for kc in range(c.NKC):
            ci = kc % 2
            xv = xkv[:, ci * c.DT * 512:(ci + 1) * c.DT * 512]
            slot = 'p0' if kc < c.NKC // 2 else 'p1'
            col0 = (kc % (c.NKC // 2)) * 512
            if kc >= 2:
                p.wait("gpsimd", f"{P}Mk", bMk + kc - 1)

            def ld(h, rg, slot=slot, col0=col0, xv=xv):
                ap = X['ago2'][DS_(rg[("gpsimd", slot)]['v'], 1), :, col0:col0 + 512]
                return h.dma_start(out=S3(xv, 512), in_=T3(ap.squeeze(0)))
            p.dyn("gpsimd", ld, inc=f"{P}Mglx")
            lxv = p.val(f"{P}Mglx")
            p.wait("tensor", f"{P}Mglx", lxv)
            for m in range(c.DIT):
                bank = ps[m % 2]
                wv_ = p.val(f"{P}Mxe") - 1
                if wv_ > bMxe:
                    p.wait("tensor", f"{P}Mxe", wv_)
                for d in range(c.DT):
                    p.op("tensor", lambda h, b=bank, m=m, d=d, xv=xv: h.matmul(
                        b[:, 0:512],
                        winb_sb[:, d * c.DI + m * 128:d * c.DI + m * 128 + 128],
                        xv[:, d * 512:(d + 1) * 512],
                        start=(d == 0), stop=(d == c.DT - 1)),
                        inc=(f"{P}Mxm" if d == c.DT - 1 else None))
                xmv = p.val(f"{P}Mxm")
                p.wait("vector", f"{P}Mxm", xmv)
                p.op("vector", lambda h, b=bank, m=m: h.tensor_copy(
                    xh[:, m * 512:(m + 1) * 512], b[:, 0:512]), inc=f"{P}Mxe")
            for m in range(c.DIT):
                bank = ps[2 + m % 2]
                wv_ = p.val(f"{P}Mze") - 1
                if wv_ > bMze:
                    p.wait("tensor", f"{P}Mze", wv_)
                for d in range(c.DT):
                    p.op("tensor", lambda h, b=bank, m=m, d=d, xv=xv: h.matmul(
                        b[:, 0:512],
                        winb_sb[:, d * c.DI + c.DIm + m * 128:
                                d * c.DI + c.DIm + m * 128 + 128],
                        xv[:, d * 512:(d + 1) * 512],
                        start=(d == 0), stop=(d == c.DT - 1)),
                        inc=(f"{P}Mzm" if d == c.DT - 1 else None))
                zmv = p.val(f"{P}Mzm")
                p.wait("scalar", f"{P}Mzm", zmv)
                p.op("scalar", lambda h, b=bank, m=m: h.activation(
                    zsb[:, m * 512:(m + 1) * 512], b[:, 0:512], AF.Silu),
                    inc=f"{P}Mze")
            if kc == 0:
                p.wait("vector", f"{P}hl0", p.val(f"{P}hl0"))
            p.op("tensor", lambda h: h.nop(), inc=f"{P}Mk")
            for m in range(c.DIT):
                wv = X['m_cw']
                p.op("vector", lambda h, m=m, wv=wv: h.tensor_scalar_mul(
                    cacc[:, 0:512], xh[:, m * 512:(m + 1) * 512],
                    wv[:, m * 4 + 3:m * 4 + 4]))
                for k in range(3):
                    w = 3 - k
                    p.op("vector", lambda h, m=m, k=k, w=w, wv=wv:
                         h.scalar_tensor_tensor(
                             cacc[:, w:512], xh[:, m * 512:(m + 1) * 512 - w],
                             wv[:, m * 4 + k:m * 4 + k + 1], cacc[:, w:512],
                             A.mult, A.add))
                    p.op("vector", lambda h, m=m, k=k, w=w, wv=wv:
                         h.scalar_tensor_tensor(
                             cacc[:, 0:w], halo[:, m * 4 + k + 1:m * 4 + k + 1 + w]
                             if False else halo[:, m * 4 + (4 - w):m * 4 + 4][:, 0:w],
                             wv[:, m * 4 + k:m * 4 + k + 1], cacc[:, 0:w],
                             A.mult, A.add))
                p.op("vector", lambda h, m=m: h.tensor_copy(
                    halo[:, m * 4 + 1:m * 4 + 4], xh[:, (m + 1) * 512 - 3:(m + 1) * 512]))
                p.op("vector", lambda h: h.nop(), inc=f"{P}Mcv")
                cvv = p.val(f"{P}Mcv")
                p.wait("scalar", f"{P}Mcv", cvv)
                p.op("scalar", lambda h, m=m: h.activation(
                    xcb[:, m * 512:(m + 1) * 512], cacc[:, 0:512], AF.Silu,
                    bias=X['m_cb'][:, m:m + 1]), inc=f"{P}Mxc")
                p.wait("vector", f"{P}Mxc", p.val(f"{P}Mxc"))
            p.wait("sync", f"{P}Mxc", p.val(f"{P}Mxc"))
            p.wait("sync", f"{P}Mze", p.val(f"{P}Mze"))
            p.dma("sync", T3(X['xcs'][:, kc * 512:(kc + 1) * 512]),
                  S3(xcb[:, :], 512), inc=f"{P}Mst")
            p.dma("sync", T3(X['zss'][:, kc * 512:(kc + 1) * 512]),
                  S3(zsb[:, :], 512), inc=f"{P}Mst")
            p.wait("tensor", f"{P}Mxc", p.val(f"{P}Mxc"))
            for m in range(c.DIT):
                p.op("tensor", lambda h, m=m: h.matmul(
                    ps[4][0:33, 0:512], wxp_sb[:, m * 33:(m + 1) * 33],
                    xcb[:, m * 512:(m + 1) * 512],
                    start=(m == 0), stop=(m == c.DIT - 1)),
                    inc=(f"{P}Mxp" if m == c.DIT - 1 else None))
            xpv = p.val(f"{P}Mxp")
            p.wait("vector", f"{P}Mxp", xpv)
            p.op("vector", lambda h: h.tensor_copy(xp33[:, :], ps[4][0:33, 0:512]),
                 inc=f"{P}Mxpe")
            p.wait("sync", f"{P}Mxpe", p.val(f"{P}Mxpe"))
            p.dma("sync", X['agc3'][:, kc * 512:(kc + 1) * 512], xp33[:, :],
                  inc=f"{P}Mst")
            p.wait("vector", f"{P}Mst", p.val(f"{P}Mst"))
            p.wait("scalar", f"{P}Mst", p.val(f"{P}Mst"))
            p.wait("scalar", f"{P}Mxp", p.val(f"{P}Mxp"))
        p.wait("gpsimd", f"{P}Mst", p.val(f"{P}Mst"))
        p.op("gpsimd", lambda h: h.collective_compute(
            "AllGather", A.bypass, replica_groups=[list(range(8))],
            ins=[X['agc3'].opt()], outs=[X['ago3'].opt()]), inc=f"{P}ag3")
        p.wait("gpsimd", f"{P}ag3", p.val(f"{P}ag3"))
        p.barrier(f"{P}barM1")
        stk1.close()
        stk2 = ExitStack()

        def sbt(name, shape, dt):
            return stk2.enter_context(nc.sbuf_tensor(f"{P}{name}", shape, dt))

        xpj = sbt("xpj", [33, c.T], F32)
        xpj2 = sbt("xpj2", [33, c.T], F32)

        def xpl(h, rg, which, dst):
            ap = X['ago3'][DS_(rg[("gpsimd", which)]['v'], 1), :, :]
            return h.dma_start(out=dst, in_=ap.squeeze(0))
        p.dyn("gpsimd", lambda h, rg: xpl(h, rg, 'p0', xpj[:, :]), inc=f"{P}Mpl")
        p.dyn("gpsimd", lambda h, rg: xpl(h, rg, 'p1', xpj2[:, :]), inc=f"{P}Mpl")
        p.wait("vector", f"{P}Mpl", p.val(f"{P}Mpl"))
        p.op("vector", lambda h: h.tensor_add(xpj[:, :], xpj[:, :], xpj2[:, :]),
             inc=f"{P}Mps")
        psv = p.val(f"{P}Mps")
        state = sbt("state", [128, c.DIT * c.DSt], F32)
        dts_ = sbt("dts", [128, c.DIT * 512], F32)
        dtx = sbt("dtx", [128, c.DIT * 512], F32)
        bcb = sbt("bcb", [128, 2 * 512], F32)
        dtr = sbt("dtr", [128, 512], F32)
        dA = sbt("dA", [128, 2 * 512], F32)
        spA = sbt("spA", [128, 2 * 512], F32)
        dBx = sbtx = sbt("dBx", [128, 2 * 512], F32)
        hsc = sbt("hsc", [128, 2 * 512], F32)
        ytl = sbt("ytl", [128, c.DIT * 512], F32)
        xcr = sbt("xcr", [128, c.DIT * 512], BF16)
        zsr = sbt("zsr", [128, c.DIT * 512], BF16)
        gto = sbt("gto", [128, c.DIT * 512], BF16)
        p.op("vector", lambda h: h.memset(state[:, :], 0.0), inc=f"{P}Ms0")
        p.wait("vector", f"{P}Mps", psv)
        p.wait("gpsimd", f"{P}Mps", psv)
        for kc in range(c.NKC):
            col = kc * 512
            p.dma("sync", S3(xcr[:, :], 512), T3(X['xcs'][:, col:col + 512]),
                  inc=f"{P}Sxl")
            p.dma("sync", S3(zsr[:, :], 512), T3(X['zss'][:, col:col + 512]),
                  inc=f"{P}Sxl")
            sxv = p.val(f"{P}Sxl")
            p.wait("tensor", f"{P}Sdt", p.val(f"{P}Sdt"))
            p.op("tensor", lambda h, col=col: h.matmul(
                ps[5][:, 0:512], X['ohott'][:, 32 * 128:33 * 128],
                xpj[0:33, col:col + 512], start=True, stop=True), inc=f"{P}Sdm")
            sdv = p.val(f"{P}Sdm")
            p.wait("scalar", f"{P}Sdm", sdv)
            p.wait("scalar", f"{P}Sdte", p.val(f"{P}Sdte"))
            p.wait("scalar", f"{P}Sdx", p.val(f"{P}Sdx"))
            p.wait("vector", f"{P}Sdm", sdv)
            p.wait("vector", f"{P}Sdte", p.val(f"{P}Sdte"))
            p.wait("vector", f"{P}Sdx", p.val(f"{P}Sdx"))
            for m in range(c.DIT):
                yv_ = dts_[:, m * 512:(m + 1) * 512]
                wdc = X['m_wdt'][:, m:m + 1]
                bdc = X['m_bdt'][:, m:m + 1]
                p.op("vector", lambda h, yv_=yv_, wdc=wdc, bdc=bdc: h.tensor_scalar(
                    yv_, ps[5][:, 0:512], wdc, bdc, A.mult, A.add),
                    inc=(f"{P}Sdy" if m == c.DIT - 1 else None))
            sdy = p.val(f"{P}Sdy")
            p.wait("scalar", f"{P}Sdy", sdy)
            for m in range(c.DIT):
                yv_ = dts_[:, m * 512:(m + 1) * 512]
                av_ = spA[:, (m % 2) * 512:(m % 2 + 1) * 512]
                if m >= 2:
                    p.wait("scalar", f"{P}Sdt", p.val(f"{P}Sdt") - 1)
                p.op("scalar", lambda h, yv_=yv_, av_=av_: h.activation(
                    av_, yv_, AF.Abs))
                p.op("scalar", lambda h, av_=av_: h.activation(
                    av_, av_, AF.Exp, scale=-1.0))
                p.op("scalar", lambda h, av_=av_: h.activation(
                    av_, av_, AF.Ln, bias=1.0), inc=f"{P}Ssp")
                spv_ = p.val(f"{P}Ssp")
                p.wait("vector", f"{P}Ssp", spv_)
                p.op("vector", lambda h, yv_=yv_: h.tensor_scalar_max(
                    yv_, yv_, 0.0))
                p.op("vector", lambda h, yv_=yv_, av_=av_: h.tensor_add(
                    yv_, yv_, av_), inc=f"{P}Sdt")
            sdt = p.val(f"{P}Sdt")
            p.wait("vector", f"{P}Sdt", sdt)
            p.wait("vector", f"{P}Sxl", sxv)
            p.wait("vector", f"{P}SdB", p.val(f"{P}SdB"))
            for m in range(c.DIT):
                p.op("vector", lambda h, m=m: h.tensor_tensor(
                    dtx[:, m * 512:(m + 1) * 512], dts_[:, m * 512:(m + 1) * 512],
                    xcr[:, m * 512:(m + 1) * 512], A.mult),
                    inc=(f"{P}Sdx" if m == c.DIT - 1 else None))
            sdx = p.val(f"{P}Sdx")
            p.wait("gpsimd", f"{P}Sdx", sdx)
            for m in range(c.DIT):
                p.op("vector", lambda h, m=m: h.tensor_scalar_mul(
                    ytl[:, m * 512:(m + 1) * 512], xcr[:, m * 512:(m + 1) * 512],
                    X['m_D'][:, m:m + 1]))
            for n in range(c.DSt):
                p.wait("tensor", f"{P}Sbe", p.val(f"{P}Sbe"))
                p.op("tensor", lambda h, n=n, col=col: h.matmul(
                    ps[6][:, 0:512], X['ohott'][:, n * 128:(n + 1) * 128],
                    xpj[0:33, col:col + 512], start=True, stop=True), inc=f"{P}Sbm")
                p.op("tensor", lambda h, n=n, col=col: h.matmul(
                    ps[7][:, 0:512], X['ohott'][:, (16 + n) * 128:(17 + n) * 128],
                    xpj[0:33, col:col + 512], start=True, stop=True), inc=f"{P}Sbm")
                bmv = p.val(f"{P}Sbm")
                p.wait("vector", f"{P}Sbm", bmv)
                p.wait("vector", f"{P}SdB", p.val(f"{P}SdB"))
                p.wait("vector", f"{P}Sy", p.val(f"{P}Sy"))
                p.op("vector", lambda h: h.tensor_copy(bcb[:, 0:512], ps[6][:, 0:512]))
                p.op("vector", lambda h: h.tensor_copy(bcb[:, 512:1024], ps[7][:, 0:512]),
                     inc=f"{P}Sbe")
                p.wait("gpsimd", f"{P}Sbe", p.val(f"{P}Sbe"))
                for m in range(c.DIT):
                    dav = dA[:, (m % 2) * 512:(m % 2 + 1) * 512]
                    p.wait("scalar", f"{P}Ssc", max(0, p.val(f"{P}Ssc") - 1))
                    p.op("scalar", lambda h, m=m, n=n, dav=dav: h.activation(
                        dav, dts_[:, m * 512:(m + 1) * 512], AF.Exp,
                        scale=X['m_A'][:, m * c.DSt + n:m * c.DSt + n + 1]),
                        inc=f"{P}SdA")
                    dxv = dBx[:, (m % 2) * 512:(m % 2 + 1) * 512]
                    p.wait("gpsimd", f"{P}Ssc", max(0, p.val(f"{P}Ssc") - 1))
                    p.op("gpsimd", lambda h, m=m, dxv=dxv: h.tensor_tensor(
                        dxv, dtx[:, m * 512:(m + 1) * 512], bcb[:, 0:512], A.mult),
                        inc=f"{P}SdB")
                    p.wait("vector", f"{P}SdA", p.val(f"{P}SdA"))
                    p.wait("vector", f"{P}SdB", p.val(f"{P}SdB"))
                    stc = state[:, m * c.DSt + n:m * c.DSt + n + 1]
                    hv_ = hsc[:, (m % 2) * 512:(m % 2 + 1) * 512]
                    p.op("vector", lambda h, hv_=hv_, dxv=dxv, stc=stc, dav=dav:
                         h.tensor_tensor_scan(hv_, dav, dxv, stc,
                                              A.mult, A.add), inc=f"{P}Ssc")
                    p.op("vector", lambda h, hv_=hv_, stc=stc: h.tensor_copy(
                        stc, hv_[:, 511:512]))
                    p.op("vector", lambda h, hv_=hv_: h.tensor_tensor(
                        hv_, hv_, bcb[:, 512:1024], A.mult))
                    p.op("vector", lambda h, hv_=hv_, m=m: h.tensor_add(
                        ytl[:, m * 512:(m + 1) * 512], ytl[:, m * 512:(m + 1) * 512],
                        hv_), inc=f"{P}Sy")
            p.op("scalar", lambda h: h.nop(), inc=f"{P}Sdte")
            for m in range(c.DIT):
                p.op("vector", lambda h, m=m: h.tensor_tensor(
                    gto[:, m * 512:(m + 1) * 512], ytl[:, m * 512:(m + 1) * 512],
                    zsr[:, m * 512:(m + 1) * 512], A.mult),
                    inc=(f"{P}Sg" if m == c.DIT - 1 else None))
            p.wait("sync", f"{P}Sg", p.val(f"{P}Sg"))
            p.dma("sync", T3(X['agc4'][:, col:col + 512]),
                  S3(gto[:, :], 512), inc=f"{P}Sst")
            p.wait("sync", f"{P}Sst", p.val(f"{P}Sst"))
        p.wait("gpsimd", f"{P}Sst", p.val(f"{P}Sst"))
        p.op("gpsimd", lambda h: h.collective_compute(
            "AllGather", A.bypass, replica_groups=[list(range(8))],
            ins=[X['agc4'].opt()], outs=[X['ago4'].opt()]), inc=f"{P}ag4")
        p.wait("gpsimd", f"{P}ag4", p.val(f"{P}ag4"))
        p.barrier(f"{P}barM2")
        stk2.close()
        stk3 = ExitStack()

        def sbt(name, shape, dt):
            return stk3.enter_context(nc.sbuf_tensor(f"{P}{name}", shape, dt))

        def gst0(h, rg):
            ap = X['ago4'][DS_(rg[("sync", "p0")]['v'], 1), :, :].squeeze(0)
            return h.dma_start(out=X['gfull'][0:c.DIm, :],
                               in_=ap[:, DS_(rg[("sync", "orow")]['v'], c.HALF)])

        def gst1(h, rg):
            ap = X['ago4'][DS_(rg[("sync", "p1")]['v'], 1), :, :].squeeze(0)
            return h.dma_start(out=X['gfull'][c.DIm:c.DI, :],
                               in_=ap[:, DS_(rg[("sync", "orow")]['v'], c.HALF)])
        p.dyn("sync", gst0, inc=f"{P}Ogs")
        p.dyn("sync", gst1, inc=f"{P}Ogs")
        p.wait("sync", f"{P}Ogs", p.val(f"{P}Ogs"))
        wo = sbt("wo", [128, c.DIT2 * c.D], BF16)
        p.dma("sync", S3(wo[:, :], c.D), T3(X['woutb']), inc=f"{P}Ow")
        gf = sbt("gf", [128, 2 * c.DIT2 * 128], BF16)
        ures = sbt("ures", [128, 2 * c.D], F32)
        x1b = sbt("x1b", [128, 2 * c.D], F32)
        p.wait("tensor", f"{P}Ow", p.val(f"{P}Ow"))
        bOmm = p.val(f"{P}Omm"); bOst = p.val(f"{P}Ost")
        for t in range(c.NT):
            bi = t % 2
            gv = gf[:, bi * c.DIT2 * 128:(bi + 1) * c.DIT2 * 128]
            uv = ures[:, bi * c.D:(bi + 1) * c.D]
            xv = x1b[:, bi * c.D:(bi + 1) * c.D]
            if t >= 2:
                p.wait("sync", f"{P}Omm", bOmm + (t - 1) * (c.D // 512))
                p.wait("sync", f"{P}Ost", bOst + (t - 1) * 16)

            p.dma("sync", S3(gv[:, :], 128),
                  T3(X['gfull'][:, t * 128:(t + 1) * 128]), inc=f"{P}Ogl")
            p.dma("sync", uv, X['mus'][t * 128:(t + 1) * 128, :], inc=f"{P}Ogl")
            p.dma("sync", xv, X['x1s'][t * 128:(t + 1) * 128, :], inc=f"{P}Ogl")
            glv = p.val(f"{P}Ogl")
            p.wait("tensor", f"{P}Ogl", glv)
            for nch in range(c.D // 512):
                bank = ps[nch % 2]
                for m in range(c.DIT2):
                    p.op("tensor", lambda h, b=bank, nch=nch, m=m, gv=gv: h.matmul(
                        b[:, 0:512], gv[:, m * 128:(m + 1) * 128],
                        wo[:, m * c.D + nch * 512:m * c.D + (nch + 1) * 512],
                        start=(m == 0), stop=(m == c.DIT2 - 1)),
                        inc=(f"{P}Omm" if m == c.DIT2 - 1 else None))
                omv = p.val(f"{P}Omm")
                p.wait("vector", f"{P}Omm", omv)
                p.op("vector", lambda h, b=bank, nch=nch, uv=uv: h.tensor_add(
                    uv[:, nch * 512:(nch + 1) * 512], b[:, 0:512],
                    uv[:, nch * 512:(nch + 1) * 512]))
                p.op("vector", lambda h, nch=nch, uv=uv, xv=xv: h.tensor_add(
                    xv[:, nch * 512:(nch + 1) * 512], uv[:, nch * 512:(nch + 1) * 512],
                    xv[:, nch * 512:(nch + 1) * 512]),
                    inc=(f"{P}Oev" if nch == c.D // 512 - 1 else None))
            p.wait("sync", f"{P}Oev", p.val(f"{P}Oev"))
            p.dma("sync", X['x2s'][t * 128:(t + 1) * 128, :], xv, inc=f"{P}Ost")
            p.wait("tensor", f"{P}Oev", p.val(f"{P}Oev"))
        p.barrier(f"{P}bar3")
        stk3.close()


def make_inputs(cfg, full, core):
    c = cfg
    b, hh = core // 2, core % 2
    x = np.asarray(full['L0_attn_x'])
    DIm = c.DIm
    ch = slice(hh * DIm, (hh + 1) * DIm)
    W_in = np.asarray(full['L0_attn_W_in'])
    winm = np.concatenate([W_in[:, hh * DIm:(hh + 1) * DIm],
                           W_in[:, c.DI + hh * DIm:c.DI + (hh + 1) * DIm]], axis=1)
    hd = c.HD
    inv = 1.0 / (10000.0 ** (np.arange(0, hd, 2, dtype=np.float64) / hd))
    tpos = np.arange(c.T, dtype=np.float64)
    fr = np.outer(tpos, inv)
    emb = np.concatenate([fr, fr], axis=1)
    cos = np.cos(emb).astype(np.float32)
    sin = np.sin(emb).astype(np.float32)
    cosr = np.tile(cos, (1, 8)).astype(np.float32)
    sinr = np.tile(sin, (1, 8)).astype(np.float32)
    kk, qq = np.meshgrid(np.arange(128), np.arange(128), indexing='ij')
    tri = np.where(kk > qq, np.float32(NEG), np.float32(0.0))
    selv = np.array([[hh * c.NT], [2 * b], [2 * b + 1 if hh == 1 else 8],
                     [2 * b], [2 * b + 1], [hh * c.HALF], [hh * c.HALF + c.HALF],
                     [0]], dtype=np.int32)
    F = np.float32
    d = {
        'x0': np.ascontiguousarray(x[b, hh * c.HALF:(hh + 1) * c.HALF, :], dtype=F),
        'qkvw_t': np.ascontiguousarray(np.asarray(full['L0_ffn_qkv_w']).T, dtype=F),
        'projw_t': np.ascontiguousarray(np.asarray(full['L0_ffn_proj_w']).T, dtype=F),
        'routerw_t': np.ascontiguousarray(np.asarray(full['L0_ffn_router_w']).T, dtype=F),
        'wg_t': np.ascontiguousarray(np.asarray(full['L0_ffn_wg']).transpose(0, 2, 1), dtype=F),
        'wu_t': np.ascontiguousarray(np.asarray(full['L0_ffn_wu']).transpose(0, 2, 1), dtype=F),
        'wd_t': np.ascontiguousarray(np.asarray(full['L0_ffn_wd']).transpose(0, 2, 1), dtype=F),
        'n1w': np.ascontiguousarray(np.asarray(full['L0_ffn_norm1_w']).reshape(1, -1), dtype=F),
        'n2w': np.ascontiguousarray(np.asarray(full['L0_ffn_norm2_w']).reshape(1, -1), dtype=F),
        'mlng': np.ascontiguousarray(np.asarray(full['L0_attn_ln_g']).reshape(1, -1), dtype=F),
        'mlnb': np.ascontiguousarray(np.asarray(full['L0_attn_ln_b']).reshape(1, -1), dtype=F),
        'winm': np.ascontiguousarray(winm, dtype=F),
        'convw': np.ascontiguousarray(np.asarray(full['L0_attn_conv_w'])[ch], dtype=F),
        'convb': np.ascontiguousarray(np.asarray(full['L0_attn_conv_b'])[ch].reshape(-1, 1), dtype=F),
        'wdtm': np.ascontiguousarray(np.asarray(full['L0_attn_W_dt'])[0, ch].reshape(-1, 1), dtype=F),
        'bdtm': np.ascontiguousarray(np.asarray(full['L0_attn_b_dt'])[ch].reshape(-1, 1), dtype=F),
        'alogm': np.ascontiguousarray(np.asarray(full['L0_attn_A_log'])[ch], dtype=F),
        'dvecm': np.ascontiguousarray(np.asarray(full['L0_attn_D'])[ch].reshape(-1, 1), dtype=F),
        'wxpm': np.ascontiguousarray(np.asarray(full['L0_attn_W_xp'])[ch], dtype=F),
        'wout': np.ascontiguousarray(np.asarray(full['L0_attn_W_out']), dtype=F),
        'cosr': cosr, 'sinr': sinr,
        'ident': np.eye(128, dtype=F),
        'ohot': np.ascontiguousarray(
            np.repeat(np.eye(33, dtype=F), 128, axis=1)),
        'tri': np.ascontiguousarray(tri),
        'selv': selv,
        'kvones': np.ascontiguousarray(
            (np.arange(c.T) < (c.HALF if hh == 0 else c.T)).astype(F).reshape(-1, 1)),
    }
    return d


_CACHED = {}


def get_program(cfg_key=None):
    if 'nc' not in _CACHED:
        cfg = CFG()
        _CACHED['cfg'] = cfg
        _CACHED['nc'] = build_program(cfg)
    return _CACHED['cfg'], _CACHED['nc']


def kernel(**inputs):
    from concourse.bass_utils import run_bass_kernel_spmd
    cfg, nc = get_program()
    in_maps = [make_inputs(cfg, inputs, core) for core in range(8)]
    res = run_bass_kernel_spmd(nc, in_maps, core_ids=list(range(8)))
    B = 4
    out = np.empty((B, cfg.T, cfg.D), dtype=np.float32)
    for core in range(8):
        b, hh = core // 2, core % 2
        out[b, hh * cfg.HALF:(hh + 1) * cfg.HALF, :] = res.results[core]["xout"]
    return out
